# version 28
# baseline (speedup 1.0000x reference)
import numpy as np
import ml_dtypes
import concourse.bass as bass
import concourse.mybir as mybir
from concourse.bass_utils import run_bass_kernel_spmd

F32 = mybir.dt.float32
I32 = mybir.dt.int32
BF16 = mybir.dt.bfloat16
AF = mybir.ActivationFunctionType
ALU = mybir.AluOpType
AX = mybir.AxisListType

P = 128
D = 1024
KC = 8
ECH = 12
NDC = 8
ST = 512
NSUB = 2
TC = 2

ECS = [c for pair in zip(range(ECH), range(ECH, 2 * ECH)) for c in pair]


class Tok:
    __slots__ = ("eng", "sem", "idx")

    def __init__(self, eng, sem, idx):
        self.eng, self.sem, self.idx = eng, sem, idx


class Sched:

    ENGS = ("sync", "gpsimd", "tensor", "vector", "scalar")

    def __init__(self):
        self.items = {e: [] for e in self.ENGS}
        self.marked = {e: set() for e in self.ENGS}
        self.sem_keys = {}

    def op(self, eng, fn, sem=None, inc=1):
        sem = sem if sem is not None else eng
        if sem not in self.sem_keys:
            self.sem_keys[sem] = inc
        assert self.sem_keys[sem] == inc
        self.items[eng].append(("op", fn, sem))
        return Tok(eng, sem, len(self.items[eng]) - 1)

    def dma(self, eng, fn, sem):
        return self.op(eng, fn, sem=sem, inc=16)

    def wait(self, eng, tok):
        if tok is None:
            return
        if tok.eng == eng and tok.sem == tok.eng:
            self.items[eng].append(("drain",))
        else:
            self.marked[tok.eng].add(tok.idx)
            self.items[eng].append(("wait", tok))

    def drain(self, eng):
        self.items[eng].append(("drain",))

    def emit(self, nc, block):
        sems = {}
        import contextlib
        stack = contextlib.ExitStack()
        for k in self.sem_keys:
            sems[k] = stack.enter_context(nc.semaphore(f"sem_{k}"))
        for e in self.ENGS:
            for i, it in enumerate(self.items[e]):
                if it[0] == "op" and self.sem_keys[it[2]] == 16:
                    self.marked[e].add(i)
        cnt = {e: [0] * len(self.items[e]) for e in self.ENGS}
        for e in self.ENGS:
            per_sem = {}
            for i, it in enumerate(self.items[e]):
                if it[0] == "op":
                    sem = it[2]
                    if i in self.marked[e]:
                        per_sem[sem] = per_sem.get(sem, 0) + 1
                    cnt[e][i] = per_sem.get(sem, 0)
        for e in self.ENGS:
            items, marked = self.items[e], self.marked[e]
            inc_of = self.sem_keys

            def body(engine, items=items, marked=marked):
                last_wait = {}
                dirty = False
                for i, it in enumerate(items):
                    if it[0] == "op":
                        inst = it[1](engine)
                        if i in marked:
                            inst.then_inc(sems[it[2]], inc_of[it[2]])
                        dirty = True
                    elif it[0] == "drain":
                        if dirty:
                            engine.drain()
                            dirty = False
                    else:
                        tok = it[1]
                        val = cnt[tok.eng][tok.idx] * inc_of[tok.sem]
                        if last_wait.get(tok.sem, -1) < val:
                            engine.wait_ge(sems[tok.sem], val)
                            last_wait[tok.sem] = val
            getattr(block, e)(body)
        return stack


def build_nc(T=2048):
    NS = T // ST
    nc = bass.Bass("TRN2", target_bir_lowering=False, dynamic_dma_scratch_size=14336)

    x_d = nc.dram_tensor("x", [T, D], F32, kind="ExternalInput").ap()
    wf_d = nc.dram_tensor("wf_s", [2 * ECH, P, KC, P], F32, kind="ExternalInput").ap()
    wd_d = nc.dram_tensor("wd_s", [NDC, P, ECH, P], F32, kind="ExternalInput").ap()
    bf_d = nc.dram_tensor("bf", [P, 2 * ECH], F32, kind="ExternalInput").ap()
    w_d = nc.dram_tensor("w", [P, NDC, D], F32, kind="ExternalInput").ap()
    fb_d = nc.dram_tensor("fb", [P, NDC], F32, kind="ExternalInput").ap()
    idf_d = nc.dram_tensor("idf", [P, P], F32, kind="ExternalInput").ap()
    idb_d = nc.dram_tensor("idb", [P, P], BF16, kind="ExternalInput").ap()
    y_d = nc.dram_tensor("y", [T, D], F32, kind="ExternalOutput").ap()
    wfhl_d = nc.dram_tensor("wf_hl", [2 * ECH, P, 2 * KC * P], BF16, kind="Internal").ap()

    import contextlib
    alloc = contextlib.ExitStack()

    def sb(name, shape, dt=F32):
        return alloc.enter_context(nc.sbuf_tensor(name, shape, dt))

    def ps(name, shape, dt=F32):
        return alloc.enter_context(nc.psum_tensor(name, shape, dt))

    wq_sb = sb("wq_sb", [P, KC, NDC, P], BF16)
    wf_buf = sb("wf_buf", [P, 2, 2, KC, P], BF16)
    wd_buf = sb("wd_buf", [P, 2, ECH, P], F32)
    xin = sb("xin", [P, 2, NSUB * TC, D], F32)
    xn_tok = sb("xn_tok", [P, TC, D], F32)
    xn_fm = sb("xn_fm", [P, 2, KC, ST], F32)

    xq_fm = sb("xq_fm", [P, 2, KC, ST], BF16)
    a_buf = sb("a_buf", [P, ECH, ST], F32)
    b2 = sb("b2", [P, 2, ST], F32)
    gtmp = sb("gtmp", [P, 2, ST], F32)
    attn_fm = sb("attn_fm", [P, NDC, ST], F32)
    ffn_fm = sb("ffn_fm", [P, NDC, ST], F32)
    _ffl = ffn_fm[:].rearrange("p a b -> p (a b)")
    wfin2 = [_ffl[:, i * KC * P:(i + 1) * KC * P].rearrange(
        "p (k c) -> p k c", c=P) for i in range(2)]
    obuf = sb("obuf", [P, 3, D], F32)
    qs_w = sb("qs_w", [P, D], F32)
    qs_f = sb("qs_f", [P, D], F32)
    qs_fi = qs_f[:].bitcast(I32)
    stats = sb("stats", [P, 32], F32)
    bf_sb = sb("bf_sb", [P, 2 * ECH], F32)
    nbf_sb = sb("nbf_sb", [P, ECH], F32)
    bfh5_sb = sb("bfh5_sb", [P, ECH], F32)
    fb_sb = sb("fb_sb", [P, NDC], F32)
    wscale = sb("wscale", [P, NDC], F32)
    xscale = sb("xscale", [P, 2, NSUB * TC], F32)
    xinv = sb("xinv", [P, NSUB * TC], F32)
    ident_f = sb("ident_f", [P, P], F32)
    ident_b = sb("ident_b", [P, P], BF16)
    carry = sb("carry", [P, ECH], F32)
    c127 = sb("c127", [P, 1], F32)
    r127 = sb("r127", [P, 1], F32)
    ntmp = sb("ntmp", [P, NSUB * TC], F32)

    psum_m = [ps(f"psum_m{i}", [P, ST]) for i in range(4)]
    psum_f = [ps("psum_f0", [P, ST]), ps("psum_f1", [P, ST])]
    psum_t = [ps("psum_t0", [P, 512]), ps("psum_t1", [P, 512])]

    MSUM, QSUM, MEAN, VAR, INV, NMI, MX, MN = 0, 4, 8, 12, 16, 20, 24, 28
    AMAX, QT = 24, 26

    S = Sched()
    t_hist = [None, None]
    t_use = [0]

    def t_bank():
        b = t_use[0] % 2
        t_use[0] += 1
        return b

    m_hist = [None] * 4
    m_use = [0]

    def m_bank():
        b = m_use[0] % 4
        m_use[0] += 1
        return b

    qlast = {n: {"t": None, "v": None, "s": None, "g": None} for n in ("w", "i", "f")}

    def claim_bufs(names, writer_eng):
        for n in names:
            for t in qlast[n].values():
                S.wait(writer_eng, t)

    qbf = obuf[:].bitcast(BF16).rearrange("p a b -> p (a b)")
    xnfm_bf = xn_fm[:].bitcast(BF16)

    def qslot(i):
        return qbf[:, i * D:(i + 1) * D]

    S.dma("gpsimd", lambda g: g.dma_start(out=bf_sb[:], in_=bf_d), "dpre")
    S.dma("gpsimd", lambda g: g.dma_start(out=fb_sb[:], in_=fb_d), "dpre")
    S.dma("gpsimd", lambda g: g.dma_start(out=ident_f[:], in_=idf_d), "dpre")
    tok_pre = S.dma("gpsimd", lambda g: g.dma_start(out=ident_b[:], in_=idb_d), "dpre")

    S.op("scalar", lambda s: s.preload_act_table(AF.Sigmoid))

    S.op("vector", lambda v: v.memset(carry[:], 0.5))
    S.op("vector", lambda v: v.memset(c127[:], 127.0))
    S.drain("vector")
    S.op("vector", lambda v: v.reciprocal(r127[:], c127[:]))
    S.wait("vector", tok_pre)
    tok_nbf = S.op("vector", lambda v: v.tensor_scalar_mul(nbf_sb[:], bf_sb[:, 0:ECH], -1.0))
    tok_bfh5 = S.op("vector", lambda v: v.tensor_scalar_add(bfh5_sb[:], bf_sb[:, ECH:2 * ECH], 0.5))

    def x_dma(u):
        def fn(g):
            return g.dma_start(
                out=xin[:, u % 2, :, :],
                in_=x_d[u * ST:(u + 1) * ST, :].rearrange("(c p) d -> p c d", p=P))
        return fn
    tok_xdma = [None] * NS
    tok_xdma[0] = S.dma("gpsimd", x_dma(0), "dx0")

    tok_stt_last = [None] * NS
    tok_out_dma = {}
    tok_wf_mm = {}
    tok_gate_s = {}
    tok_gate_v = {}
    tok_lin_v = {}
    tok_scan = {}
    tok_carry = {}
    tok_wd_mm = {}
    tok_attn_ev = {}
    tok_ffn_mm = {}
    tok_ffn_ev = {}
    tok_outT_ffn_last = [None] * NS
    tok_outT_attn_last = [None] * NS
    tok_xnT_last = {}
    tok_xnfm_ev = [None] * NS
    tok_xqfm_ev = [None] * NS
    tok_xn_reader_v = {}
    tok_xn_last = {}
    tok_wq_ready = None
    tok_wscale_last = None
    tok_q = {}
    tok_xqT_last_super = {}
    tok_wqT_last = [None]
    slot_tlast = [None] * 6

    def stage_BC(u):
        pbuf = u % 2
        S.wait("vector", tok_xdma[u])
        if u > 0:
            S.wait("vector", tok_xn_last.get((u - 1, NSUB - 1)))
        for sc in range(NSUB * TC):
            xsrc = xin[:, pbuf, sc, :]
            S.op("vector", lambda v, xsrc=xsrc, sc=sc: v.reduce_sum(
                stats[:, MSUM + sc:MSUM + sc + 1], xsrc, axis=AX.X))
            S.op("vector", lambda v, xsrc=xsrc, sc=sc: v.reduce_max(
                stats[:, MX + sc:MX + sc + 1], xsrc, axis=AX.X))
            S.op("vector", lambda v, xsrc=xsrc, sc=sc: v.tensor_reduce(
                stats[:, MN + sc:MN + sc + 1], xsrc, axis=AX.X, op=ALU.min))
            if sc % 2:
                yield
        for sc in range(NSUB * TC):
            xsrc = xin[:, pbuf, sc, :]
            name = "w" if sc % 2 == 0 else "f"
            buf = qs_w if sc % 2 == 0 else qs_f
            claim_bufs([name], "scalar")
            S.wait("scalar", tok_xdma[u])
            tsq = S.op("scalar", lambda s, xsrc=xsrc, buf=buf: s.activation(buf[:], xsrc, AF.Square))
            qlast[name]["s"] = tsq
            S.wait("vector", tsq)
            tqs = S.op("vector", lambda v, sc=sc, buf=buf: v.reduce_sum(
                stats[:, QSUM + sc:QSUM + sc + 1], buf[:], axis=AX.X))
            qlast[name]["v"] = tqs
            yield

        def vop(fn):
            t = S.op("vector", fn)
            S.wait("vector", t)
            return t
        NC4 = NSUB * TC
        vop(lambda v: v.tensor_scalar_mul(stats[:, MEAN:MEAN + NC4], stats[:, MSUM:MSUM + NC4], 1.0 / D))
        vop(lambda v: v.tensor_scalar_mul(stats[:, QSUM:QSUM + NC4], stats[:, QSUM:QSUM + NC4], 1.0 / D))
        vop(lambda v: v.tensor_mul(stats[:, VAR:VAR + NC4], stats[:, MEAN:MEAN + NC4], stats[:, MEAN:MEAN + NC4]))
        vop(lambda v: v.tensor_sub(stats[:, VAR:VAR + NC4], stats[:, QSUM:QSUM + NC4], stats[:, VAR:VAR + NC4]))
        vop(lambda v: v.tensor_scalar_add(stats[:, VAR:VAR + NC4], stats[:, VAR:VAR + NC4], 1e-5))
        vop(lambda v: v.reciprocal(stats[:, INV:INV + NC4], stats[:, VAR:VAR + NC4]))
        yield
        for _ in range(4):
            vop(lambda v: v.tensor_mul(ntmp[:], stats[:, INV:INV + NC4], stats[:, INV:INV + NC4]))
            vop(lambda v: v.tensor_mul(ntmp[:], ntmp[:], stats[:, VAR:VAR + NC4]))
            vop(lambda v: v.tensor_scalar(ntmp[:], ntmp[:], -0.5, 1.5, op0=ALU.mult, op1=ALU.add))
            vop(lambda v: v.tensor_mul(stats[:, INV:INV + NC4], stats[:, INV:INV + NC4], ntmp[:]))
            yield
        tok_nmi = vop(lambda v: v.scalar_tensor_tensor(
            stats[:, NMI:NMI + NC4], stats[:, MEAN:MEAN + NC4], -1.0, stats[:, INV:INV + NC4],
            op0=ALU.mult, op1=ALU.mult))
        vop(lambda v: v.tensor_sub(stats[:, MX:MX + NC4], stats[:, MX:MX + NC4], stats[:, MEAN:MEAN + NC4]))
        vop(lambda v: v.tensor_sub(stats[:, MN:MN + NC4], stats[:, MEAN:MEAN + NC4], stats[:, MN:MN + NC4]))
        vop(lambda v: v.tensor_max(stats[:, MX:MX + NC4], stats[:, MX:MX + NC4], stats[:, MN:MN + NC4]))
        vop(lambda v: v.tensor_mul(stats[:, MX:MX + NC4], stats[:, MX:MX + NC4], stats[:, INV:INV + NC4]))
        vop(lambda v: v.tensor_scalar_max(stats[:, MX:MX + NC4], stats[:, MX:MX + NC4], 1e-8))
        tok_xs = vop(lambda v, pbuf=pbuf: v.tensor_scalar_mul(
            xscale[:, pbuf, 0:NC4], stats[:, MX:MX + NC4], r127[:]))
        vop(lambda v, pbuf=pbuf: v.reciprocal(xinv[:, 0:NC4], xscale[:, pbuf, 0:NC4]))
        vop(lambda v: v.tensor_mul(stats[:, MN:MN + NC4], stats[:, INV:INV + NC4], xinv[:, 0:NC4]))
        tok_qaff = vop(lambda v: v.scalar_tensor_tensor(
            stats[:, MSUM:MSUM + NC4], stats[:, NMI:NMI + NC4], 0.5, xinv[:, 0:NC4],
            op0=ALU.bypass, op1=ALU.mult))
        tok_qaff = vop(lambda v: v.tensor_scalar_add(
            stats[:, MSUM:MSUM + NC4], stats[:, MSUM:MSUM + NC4], 0.5))
        yield

        last_xn_ev = None
        last_xq_ev = None
        for sub in range(NSUB):
            prev_key = (u, sub - 1) if sub else (u - 1, NSUB - 1)
            for tc in range(TC):
                sc = sub * TC + tc
                xsrc = xin[:, pbuf, sc, :]
                col = sub * 256 + tc * P
                S.wait("scalar", tok_nmi)
                S.wait("scalar", tok_xnT_last.get(prev_key))
                tok_xn = S.op("scalar", lambda s, xsrc=xsrc, tc=tc, sc=sc: s.activation(
                    xn_tok[:, tc, :], xsrc, AF.Identity,
                    bias=stats[:, NMI + sc:NMI + sc + 1], scale=stats[:, INV + sc:INV + sc + 1]))
                tok_xn_last[(u, sub)] = tok_xn
                yield
                name = "w" if tc % 2 == 0 else "f"
                buf = qs_w if tc % 2 == 0 else qs_f
                bv = buf[:].bitcast(BF16)
                claim_bufs([name], "scalar")
                tch = S.op("scalar", lambda s, bv=bv, xsrc=xsrc, sc=sc: s.activation(
                    bv[:, 0:D], xsrc, AF.Identity,
                    bias=stats[:, NMI + sc:NMI + sc + 1], scale=stats[:, INV + sc:INV + sc + 1]))
                S.wait("vector", tch)
                tsp = S.op("vector", lambda v, bv=bv, tc=tc: v.tensor_tensor(
                    bv[:, D:2 * D], xn_tok[:, tc, :], bv[:, 0:D], op=ALU.subtract))
                qlast[name]["v"] = tsp
                qlast[name]["s"] = tch
                yield
                for hl in range(2):
                    for hh in range(2):
                        bk = t_bank()
                        S.wait("tensor", tsp)
                        S.wait("tensor", tok_pre)
                        S.wait("tensor", t_hist[bk])
                        pb = psum_t[bk][:].bitcast(BF16)
                        tg = None
                        for s4 in range(4):
                            kc = hh * 4 + s4
                            tg = S.op("tensor", lambda t, pb=pb, s4=s4, kc=kc, hl=hl, bv=bv: t.transpose(
                                pb[:, s4 * P:(s4 + 1) * P],
                                bv[:, hl * D + kc * P:hl * D + (kc + 1) * P], ident_b[:]))
                        tok_xnT_last[(u, sub)] = tg
                        qlast[name]["t"] = tg
                        S.wait("scalar", tg)
                        if u > 1 and sub == 0 and tc == 0 and hl == 0 and hh == 0:
                            S.wait("scalar", tok_wf_mm[(u - 2, 2 * ECH - 1)])
                        tok_ev = S.op("scalar", lambda s, pb=pb, hh=hh, col=col, pbuf=pbuf, hl=hl: s.copy(
                            xnfm_bf[:, pbuf, hh * 4:(hh + 1) * 4, hl * ST + col:hl * ST + col + P],
                            pb[:, 0:512].rearrange("p (a b) -> p a b", b=P)))
                        t_hist[bk] = tok_ev
                        last_xn_ev = tok_ev
                    yield
        for sc in range(NSUB * TC):
            xsrc = xin[:, pbuf, sc, :]
            s2c = stats[:, MN + sc:MN + sc + 1]
            b5c = stats[:, MSUM + sc:MSUM + sc + 1]
            claim_bufs(["w", "f"], "vector")
            S.wait("vector", tok_qaff)
            S.op("vector", lambda v, xsrc=xsrc, s2c=s2c, b5c=b5c: v.tensor_scalar(
                qs_w[:], xsrc, s2c, b5c, op0=ALU.mult, op1=ALU.add))
            S.op("vector", lambda v, xsrc=xsrc, s2c=s2c, b5c=b5c: v.tensor_scalar(
                qs_fi[:], xsrc, s2c, b5c, op0=ALU.mult, op1=ALU.add))
            S.drain("vector")
            S.op("vector", lambda v: v.tensor_tensor(
                qs_w[:], qs_w[:], qs_fi[:], op=ALU.is_lt))
            S.drain("vector")
            S.wait("vector", slot_tlast[sc])
            if u > 0:
                pck = (u - 1) * 4 + sc
                if pck in tok_out_dma:
                    S.wait("vector", tok_out_dma[pck])
            tok_q[(u, sc)] = S.op("vector", lambda v, sc=sc: v.tensor_tensor(
                qslot(sc), qs_fi[:], qs_w[:], op=ALU.subtract))
            for n in ("w", "f"):
                qlast[n]["v"] = tok_q[(u, sc)]
            yield
        for sc in range(NSUB * TC):
            col = sc * P
            for hh in range(2):
                bk = t_bank()
                S.wait("tensor", tok_q[(u, sc)])
                S.wait("tensor", t_hist[bk])
                pb = psum_t[bk][:].bitcast(BF16)
                tg = None
                for s4 in range(4):
                    kc = hh * 4 + s4
                    tg = S.op("tensor", lambda t, pb=pb, s4=s4, kc=kc, sc=sc: t.transpose(
                        pb[:, s4 * P:(s4 + 1) * P],
                        qslot(sc)[:, kc * P:(kc + 1) * P], ident_b[:]))
                tok_xqT_last_super[u] = tg
                slot_tlast[sc] = tg
                S.wait("scalar", tg)
                if u > 1 and sc == 0 and hh == 0:
                    S.wait("scalar", tok_ffn_mm[(u - 2, NDC - 1)])
                tok_ev = S.op("scalar", lambda s, pb=pb, hh=hh, col=col, pbuf=pbuf: s.copy(
                    xq_fm[:, pbuf, hh * 4:(hh + 1) * 4, col:col + P],
                    pb[:, 0:512].rearrange("p (a b) -> p a b", b=P)))
                t_hist[bk] = tok_ev
                last_xq_ev = tok_ev
                yield
        tok_xnfm_ev[u] = last_xn_ev
        tok_xqfm_ev[u] = last_xq_ev

    tok_wsplit = {}
    tok_wsout = {}

    def stage_D(u, feed):
        pbuf = u % 2
        for j in range(2 * ECH):
            feed(2 if u == 0 else 1)
            sg = u * 2 * ECH + j
            e2 = j // 2
            par = j % 2
            if u == 0:
                wfin = wfin2[par]
                if sg >= 2:
                    S.wait("sync", tok_wsplit[sg - 2])
                tok_dwf = S.dma("sync", lambda s, j=j, wfin=wfin: s.dma_start(
                    out=wfin, in_=wf_d[ECS[j]]), f"dwin{par}")
                if sg >= 2:
                    pi, pj = divmod(sg - 2, 2 * ECH)
                    S.wait("gpsimd", tok_wf_mm[(pi, pj)])
                    S.wait("gpsimd", tok_wsout[j - 2])
                tcw = S.dma("gpsimd", lambda g, j=j, par=par: g.dma_start(
                    out=wf_buf[:, par, 0, :, :], in_=wf_d[ECS[j]]), f"dcw{par}")
                S.wait("vector", tcw)
                S.wait("vector", tok_dwf)
                tsp = S.op("vector", lambda v, par=par, wfin=wfin: v.tensor_tensor(
                    wf_buf[:, par, 1, :, :], wfin, wf_buf[:, par, 0, :, :],
                    op=ALU.subtract))
                tok_wsplit[sg] = tsp
                S.wait("sync", tsp)
                tok_wsout[j] = S.dma("sync", lambda s, j=j, par=par: s.dma_start(
                    out=wfhl_d[ECS[j]], in_=wf_buf[:, par, :, :, :].rearrange(
                        "p a b c -> p (a b c)")), f"dws{par}")
                tok_wrdy = tsp
            else:
                if sg >= 2:
                    pi, pj = divmod(sg - 2, 2 * ECH)
                    S.wait("sync", tok_wf_mm[(pi, pj)])
                if u == 1:
                    S.wait("sync", tok_wsout[2 * ECH - 2 + par])
                tok_wrdy = S.dma("sync", lambda s, j=j, par=par: s.dma_start(
                    out=wf_buf[:, par, :, :, :].rearrange("p a b c -> p (a b c)"),
                    in_=wfhl_d[ECS[j]]), f"dwf{par}")
            mb = m_bank()
            if u == 0:
                S.wait("tensor", tcw)
                S.wait("tensor", tok_xnfm_ev[u])
                S.wait("tensor", m_hist[mb])
                for kc in range(KC):
                    S.op("tensor", lambda t, kc=kc, mb=mb, pbuf=pbuf, par=par: t.matmul(
                        psum_m[mb][:], wf_buf[:, par, 0, kc, :],
                        xnfm_bf[:, pbuf, kc, 0:ST],
                        start=(kc == 0), stop=False))
                S.wait("tensor", tok_wrdy)
                tg = None
                for kc in range(KC):
                    for (wl, xl) in ((0, 1), (1, 0)):
                        tg = S.op("tensor", lambda t, kc=kc, mb=mb, pbuf=pbuf, par=par, wl=wl, xl=xl: t.matmul(
                            psum_m[mb][:], wf_buf[:, par, wl, kc, :],
                            xnfm_bf[:, pbuf, kc, xl * ST:(xl + 1) * ST],
                            start=False,
                            stop=(kc == KC - 1 and wl == 1)))
            else:
                S.wait("tensor", tok_wrdy)
                S.wait("tensor", tok_xnfm_ev[u])
                S.wait("tensor", m_hist[mb])
                tg = None
                for kc in range(KC):
                    for (wl, xl) in ((0, 0), (0, 1), (1, 0)):
                        tg = S.op("tensor", lambda t, kc=kc, mb=mb, pbuf=pbuf, par=par, wl=wl, xl=xl: t.matmul(
                            psum_m[mb][:], wf_buf[:, par, wl, kc, :],
                            xnfm_bf[:, pbuf, kc, xl * ST:(xl + 1) * ST],
                            start=(kc == 0 and wl == 0 and xl == 0),
                            stop=(kc == KC - 1 and wl == 1)))
            tok_wf_mm[(u, j)] = tg
            S.wait("scalar", tg)
            if j % 2 == 0:
                bpar = e2 % 2
                if u > 0:
                    S.wait("scalar", tok_carry[u - 1])
                    S.wait("scalar", tok_wd_mm[(u - 1, NDC - 1)])
                if e2 >= 2:
                    S.wait("scalar", tok_scan[(u, e2 - 2)])
                elif u > 0:
                    S.wait("scalar", tok_scan[(u - 1, ECH - 2 + e2)])
                S.wait("scalar", tok_pre)
                S.wait("scalar", tok_nbf)
                S.op("scalar", lambda s, mb=mb, e2=e2, bpar=bpar: s.activation(
                    b2[:, bpar, :], psum_m[mb][:], AF.Sigmoid, bias=bf_sb[:, e2:e2 + 1]))
                tok_gate_s[(u, j)] = S.op("scalar", lambda s, mb=mb, e2=e2: s.activation(
                    a_buf[:, e2, :], psum_m[mb][:], AF.Sigmoid,
                    bias=nbf_sb[:, e2:e2 + 1], scale=-1.0))
            else:
                q = e2 % 2
                bpar = e2 % 2
                if e2 >= 2:
                    S.wait("scalar", tok_gate_v[(u, e2 - 2)])
                elif u > 0:
                    S.wait("scalar", tok_gate_v[(u - 1, ECH - 2 + e2)])
                tok_gate_s[(u, j)] = S.op("scalar", lambda s, mb=mb, q=q, e2=e2: s.activation(
                    gtmp[:, q, :], psum_m[mb][:], AF.Sigmoid, bias=bf_sb[:, ECH + e2:ECH + e2 + 1]))
                S.wait("vector", tok_gate_s[(u, j)])
                S.wait("vector", tok_bfh5)
                tv1 = S.op("vector", lambda v, mb=mb, q=q, e2=e2: v.scalar_tensor_tensor(
                    gtmp[:, q, :], psum_m[mb][:], bfh5_sb[:, e2:e2 + 1], gtmp[:, q, :],
                    op0=ALU.add, op1=ALU.max))
                tok_lin_v[(u, e2)] = tv1
                S.wait("vector", tv1)
                S.wait("vector", tok_gate_s[(u, j - 1)])
                tok_gate_v[(u, e2)] = S.op("vector", lambda v, q=q, e2=e2, bpar=bpar: v.tensor_mul(
                    b2[:, bpar, :], b2[:, bpar, :], gtmp[:, q, :]))
                S.wait("vector", tok_gate_v[(u, e2)])
                if u > 0:
                    S.wait("vector", tok_wd_mm[(u - 1, NDC - 1)])
                    S.wait("vector", tok_carry[u - 1])
                tok_scan[(u, e2)] = S.op("vector", lambda v, e2=e2, bpar=bpar: v.tensor_tensor_scan(
                    a_buf[:, e2, :], a_buf[:, e2, :], b2[:, bpar, :], carry[:, e2:e2 + 1],
                    op0=ALU.mult, op1=ALU.add))
                if e2 == ECH - 1:
                    S.wait("vector", tok_scan[(u, e2)])
                    tok_carry[u] = S.op("vector", lambda v: v.tensor_copy(
                        carry[:, 0:ECH],
                        a_buf[:, :, ST - 1:ST].rearrange("p a b -> p (a b)")))
            m_hist[mb] = tok_gate_s[(u, j)] if j % 2 == 0 else tok_lin_v[(u, e2)]

    def stage_WQ():
        nonlocal tok_wq_ready, tok_wscale_last
        xw4 = xin[:, 1, :, :]
        aw4 = attn_fm[:].rearrange("p a b -> p (a b)").rearrange(
            "p (a b) -> p a b", b=D)
        tok_dw0 = S.dma("sync", lambda e: e.dma_start(
            out=xw4, in_=w_d[:, 0:4, :]), "dwq0")
        tok_dw1 = S.dma("sync", lambda e: e.dma_start(
            out=aw4, in_=w_d[:, 4:8, :]), "dwq1")
        last_w_reader = None
        WV = 4
        S.wait("vector", tok_dw0)
        S.wait("vector", tok_dw1)
        for dc in range(NDC):
            wsrc = (xw4 if dc < 4 else aw4)[:, dc % 4, :]
            S.op("vector", lambda v, wsrc=wsrc, dc=dc: v.reduce_max(
                stats[:, WV + dc:WV + dc + 1], wsrc, axis=AX.X, apply_absolute_value=True))
            if dc % 4 == 3:
                yield
        S.drain("vector")
        t2 = S.op("vector", lambda v: v.tensor_scalar_max(
            stats[:, WV:WV + NDC], stats[:, WV:WV + NDC], 1e-8))
        S.wait("vector", t2)
        t3 = S.op("vector", lambda v: v.tensor_scalar_mul(
            wscale[:, 0:NDC], stats[:, WV:WV + NDC], r127[:]))
        tok_wscale_last = t3
        S.wait("vector", t3)
        t4 = S.op("vector", lambda v: v.reciprocal(
            stats[:, WV:WV + NDC], wscale[:, 0:NDC]))
        S.wait("vector", t4)
        yield
        for dc in range(NDC):
            wsrc = (xw4 if dc < 4 else aw4)[:, dc % 4, :]
            winv = stats[:, WV + dc:WV + dc + 1]
            claim_bufs(["w", "f"], "vector")
            S.op("vector", lambda v, wsrc=wsrc, winv=winv: v.tensor_scalar(
                qs_w[:], wsrc, winv, 0.5, op0=ALU.mult, op1=ALU.add))
            ti = S.op("vector", lambda v, wsrc=wsrc, winv=winv: v.tensor_scalar(
                qs_fi[:], wsrc, winv, 0.5, op0=ALU.mult, op1=ALU.add))
            last_w_reader = ti
            S.drain("vector")
            S.op("vector", lambda v: v.tensor_tensor(
                qs_w[:], qs_w[:], qs_fi[:], op=ALU.is_lt))
            S.drain("vector")
            wslot = dc % 6
            S.wait("vector", slot_tlast[wslot])
            tq = S.op("vector", lambda v, wslot=wslot: v.tensor_tensor(
                qslot(wslot), qs_fi[:], qs_w[:], op=ALU.subtract))
            for n in ("w", "f"):
                qlast[n]["v"] = tq
            yield
            for hh in range(2):
                bk = t_bank()
                S.wait("tensor", tq)
                S.wait("tensor", t_hist[bk])
                pb = psum_t[bk][:].bitcast(BF16)
                tg = None
                for s4 in range(4):
                    kc = hh * 4 + s4
                    tg = S.op("tensor", lambda t, pb=pb, s4=s4, kc=kc, wslot=wslot: t.transpose(
                        pb[:, s4 * P:(s4 + 1) * P],
                        qslot(wslot)[:, kc * P:(kc + 1) * P], ident_b[:]))
                slot_tlast[wslot] = tg
                tok_wqT_last[0] = tg
                S.wait("scalar", tg)
                tok_ev = S.op("scalar", lambda s, pb=pb, hh=hh, dc=dc: s.copy(
                    wq_sb[:, hh * 4:(hh + 1) * 4, dc, :],
                    pb[:, 0:512].rearrange("p (a b) -> p a b", b=P)))
                t_hist[bk] = tok_ev
                tok_wq_ready = tok_ev
                yield
        if NS > 1:
            S.wait("gpsimd", last_w_reader)
            tok_xdma[1] = S.dma("gpsimd", x_dma(1), "dx1")

    def stage_F(u, feed):
        pbuf = u % 2
        for dc in range(NDC):
            feed(6)
            fpar = dc % 2
            S.wait("tensor", tok_xqfm_ev[u])
            S.wait("tensor", tok_wq_ready)
            if dc >= 2:
                S.wait("tensor", tok_ffn_ev[(u, dc - 2)])
            elif u > 0:
                S.wait("tensor", tok_ffn_ev[(u - 1, NDC - 2 + dc)])
            tg = None
            for kc in range(KC):
                tg = S.op("tensor", lambda t, fpar=fpar, kc=kc, dc=dc, pbuf=pbuf: t.matmul(
                    psum_f[fpar][:], wq_sb[:, kc, dc, :], xq_fm[:, pbuf, kc, :],
                    start=(kc == 0), stop=(kc == KC - 1)))
            tok_ffn_mm[(u, dc)] = tg
            S.wait("scalar", tg)
            S.wait("scalar", tok_wscale_last)
            if u > 0:
                S.wait("scalar", tok_outT_ffn_last[u - 1])
            tok_ffn_ev[(u, dc)] = S.op("scalar", lambda s, fpar=fpar, dc=dc: s.activation(
                ffn_fm[:, dc, :], psum_f[fpar][:], AF.Copy, scale=wscale[:, dc:dc + 1]))

    def stage_E(u, feed):
        for dc in range(NDC):
            sgd = u * NDC + dc
            if sgd >= 2:
                pi, pdc = divmod(sgd - 2, NDC)
                S.wait("sync", tok_wd_mm[(pi, pdc)])
            tok_dwd = S.dma("sync", lambda s, dc=dc: s.dma_start(
                out=wd_buf[:, dc % 2, :, :], in_=wd_d[dc]), f"dwd{dc % 2}")
            mb = m_bank()
            S.wait("tensor", tok_dwd)
            S.wait("tensor", tok_carry[u])
            S.wait("tensor", m_hist[mb])
            tg = None
            for e2 in range(ECH):
                tg = S.op("tensor", lambda t, dc=dc, e2=e2, mb=mb: t.matmul(
                    psum_m[mb][:], wd_buf[:, dc % 2, e2, :], a_buf[:, e2, :],
                    start=(e2 == 0), stop=(e2 == ECH - 1)))
            tok_wd_mm[(u, dc)] = tg
            S.wait("scalar", tg)
            S.wait("scalar", tok_pre)
            if u > 0:
                S.wait("scalar", tok_outT_attn_last[u - 1])
            tok_attn_ev[(u, dc)] = S.op("scalar", lambda s, mb=mb, dc=dc: s.activation(
                attn_fm[:, dc, :], psum_m[mb][:], AF.Identity, bias=fb_sb[:, dc:dc + 1]))
            m_hist[mb] = tok_attn_ev[(u, dc)]

    def stage_G(u):
        pbuf = u % 2
        for sub in range(NSUB):
            for tc in range(TC):
                col = sub * 256 + tc * P
                sc = sub * TC + tc
                ck = u * 4 + sub * TC + tc
                r = ck % 3
                tok_add = None
                for j2 in range(2):
                    bk = t_bank()
                    S.wait("tensor", tok_ffn_ev[(u, j2 * 4 + 3)])
                    S.wait("tensor", t_hist[bk])
                    tgf = None
                    for s4 in range(4):
                        dc = j2 * 4 + s4
                        tgf = S.op("tensor", lambda t, bk=bk, s4=s4, dc=dc, col=col: t.transpose(
                            psum_t[bk][:, s4 * P:(s4 + 1) * P],
                            ffn_fm[:, dc, col:col + P], ident_f[:]))
                    tok_outT_ffn_last[u] = tgf
                    S.wait("vector", tgf)
                    if ck >= 3:
                        S.wait("vector", tok_out_dma[ck - 3])
                    S.wait("vector", tok_xqT_last_super.get(u + 1))
                    S.wait("vector", tok_wqT_last[0])
                    tok_stt = S.op("vector", lambda v, bk=bk, r=r, j2=j2, pbuf=pbuf, sub=sub, tc=tc, sc=sc: v.scalar_tensor_tensor(
                        obuf[:, r, j2 * 512:(j2 + 1) * 512], psum_t[bk][:],
                        xscale[:, pbuf, sc:sc + 1], xin[:, pbuf, sub * TC + tc, j2 * 512:(j2 + 1) * 512],
                        op0=ALU.mult, op1=ALU.add))
                    t_hist[bk] = tok_stt
                    tok_stt_last[u] = tok_stt
                    bk2 = t_bank()
                    S.wait("tensor", tok_attn_ev[(u, j2 * 4 + 3)])
                    S.wait("tensor", t_hist[bk2])
                    tga = None
                    for s4 in range(4):
                        dc = j2 * 4 + s4
                        tga = S.op("tensor", lambda t, bk2=bk2, s4=s4, dc=dc, col=col: t.transpose(
                            psum_t[bk2][:, s4 * P:(s4 + 1) * P],
                            attn_fm[:, dc, col:col + P], ident_f[:]))
                    tok_outT_attn_last[u] = tga
                    S.wait("vector", tga)
                    S.wait("vector", tok_stt)
                    tok_add = S.op("vector", lambda v, bk2=bk2, r=r, j2=j2: v.tensor_add(
                        obuf[:, r, j2 * 512:(j2 + 1) * 512],
                        obuf[:, r, j2 * 512:(j2 + 1) * 512], psum_t[bk2][:]))
                    t_hist[bk2] = tok_add
                S.wait("gpsimd", tok_add)
                t0 = u * ST + sub * 256 + tc * P
                tok_out_dma[ck] = S.dma("gpsimd", lambda g, t0=t0, r=r: g.dma_start(
                    out=y_d[t0:t0 + P, :], in_=obuf[:, r, :]), f"dout{r}")

    filler = []

    def feed(n):
        while n > 0 and filler:
            try:
                next(filler[0])
                n -= 1
            except StopIteration:
                filler.pop(0)

    def exhaust():
        while filler:
            try:
                next(filler[0])
            except StopIteration:
                filler.pop(0)

    g = stage_BC(0)
    for _ in g:
        pass
    for u in range(NS):
        if u == 0:
            filler.append(stage_WQ())
        if u + 1 < NS:
            filler.append(stage_BC(u + 1))
        stage_D(u, feed)
        if u == 0 and filler:
            wq_gen = filler[0]
            for _ in wq_gen:
                pass
            filler.pop(0)
        stage_F(u, feed)
        stage_E(u, feed)
        exhaust()
        stage_G(u)
        if u + 2 < NS:
            S.wait("gpsimd", tok_stt_last[u])
            tok_xdma[u + 2] = S.dma("gpsimd", x_dma(u + 2), f"dx{(u + 2) % 2}")

    for ck in range(max(0, NS * 4 - 3), NS * 4):
        S.wait("gpsimd", tok_out_dma[ck])

    with nc.Block() as block:
        guard = S.emit(nc, block)
    guard.close()
    alloc.close()
    return nc


def _layout(L0_attn_x, L0_attn_W_f, L0_attn_b_f, L0_attn_W_down, L0_ffn_weight, L0_ffn_bias):
    x = np.ascontiguousarray(np.asarray(L0_attn_x, dtype=np.float32))
    wf = np.asarray(L0_attn_W_f, dtype=np.float32)
    bfv = np.asarray(L0_attn_b_f, dtype=np.float32)
    wd = np.asarray(L0_attn_W_down, dtype=np.float32)
    w = np.asarray(L0_ffn_weight, dtype=np.float32)
    fb = np.asarray(L0_ffn_bias, dtype=np.float32)

    wf_s = np.ascontiguousarray(wf.T.reshape(KC, P, 2 * ECH, P).transpose(2, 1, 0, 3))
    wd_s = np.ascontiguousarray(wd.T.reshape(ECH, P, NDC, P).transpose(2, 1, 0, 3))
    bf_h = np.ascontiguousarray(bfv.reshape(2 * ECH, P).T)
    w_h = np.ascontiguousarray(w.reshape(NDC, P, D).transpose(1, 0, 2))
    fb_h = np.ascontiguousarray(fb.reshape(NDC, P).T)
    idf = np.eye(P, dtype=np.float32)
    idb = np.eye(P, dtype=ml_dtypes.bfloat16)
    return x, wf_s, wd_s, bf_h, w_h, fb_h, idf, idb


def kernel(L0_attn_x, L0_attn_W_f, L0_attn_b_f, L0_attn_W_down, L0_ffn_weight, L0_ffn_bias):
    x, wf_s, wd_s, bf_h, w_h, fb_h, idf, idb = _layout(
        L0_attn_x, L0_attn_W_f, L0_attn_b_f, L0_attn_W_down, L0_ffn_weight, L0_ffn_bias)
    B, T, _ = x.shape
    nc = build_nc(T)
    in_maps = [dict(x=np.ascontiguousarray(x[b]), wf_s=wf_s, wd_s=wd_s, bf=bf_h,
                    w=w_h, fb=fb_h, idf=idf, idb=idb) for b in range(B)]
    res = run_bass_kernel_spmd(nc, in_maps, core_ids=list(range(B)))
    return np.stack([res.results[b]["y"] for b in range(B)], axis=0)


# version 29 (final)
# speedup vs baseline: 1.0158x; 1.0158x over previous
import numpy as np
import ml_dtypes
import concourse.bass as bass
import concourse.mybir as mybir
from concourse.bass_utils import run_bass_kernel_spmd

F32 = mybir.dt.float32
I32 = mybir.dt.int32
BF16 = mybir.dt.bfloat16
AF = mybir.ActivationFunctionType
ALU = mybir.AluOpType
AX = mybir.AxisListType

P = 128
D = 1024
KC = 8
ECH = 12
NDC = 8
ST = 512
NSUB = 2
TC = 2

ECS = [c for pair in zip(range(ECH), range(ECH, 2 * ECH)) for c in pair]


class Tok:
    __slots__ = ("eng", "sem", "idx")

    def __init__(self, eng, sem, idx):
        self.eng, self.sem, self.idx = eng, sem, idx


class Sched:

    ENGS = ("sync", "gpsimd", "tensor", "vector", "scalar")

    def __init__(self):
        self.items = {e: [] for e in self.ENGS}
        self.marked = {e: set() for e in self.ENGS}
        self.sem_keys = {}

    def op(self, eng, fn, sem=None, inc=1):
        sem = sem if sem is not None else eng
        if sem not in self.sem_keys:
            self.sem_keys[sem] = inc
        assert self.sem_keys[sem] == inc
        self.items[eng].append(("op", fn, sem))
        return Tok(eng, sem, len(self.items[eng]) - 1)

    def dma(self, eng, fn, sem):
        return self.op(eng, fn, sem=sem, inc=16)

    def wait(self, eng, tok):
        if tok is None:
            return
        if tok.eng == eng and tok.sem == tok.eng:
            self.items[eng].append(("drain",))
        else:
            self.marked[tok.eng].add(tok.idx)
            self.items[eng].append(("wait", tok))

    def drain(self, eng):
        self.items[eng].append(("drain",))

    def emit(self, nc, block):
        sems = {}
        import contextlib
        stack = contextlib.ExitStack()
        for k in self.sem_keys:
            sems[k] = stack.enter_context(nc.semaphore(f"sem_{k}"))
        for e in self.ENGS:
            for i, it in enumerate(self.items[e]):
                if it[0] == "op" and self.sem_keys[it[2]] == 16:
                    self.marked[e].add(i)
        cnt = {e: [0] * len(self.items[e]) for e in self.ENGS}
        for e in self.ENGS:
            per_sem = {}
            for i, it in enumerate(self.items[e]):
                if it[0] == "op":
                    sem = it[2]
                    if i in self.marked[e]:
                        per_sem[sem] = per_sem.get(sem, 0) + 1
                    cnt[e][i] = per_sem.get(sem, 0)
        for e in self.ENGS:
            items, marked = self.items[e], self.marked[e]
            inc_of = self.sem_keys

            def body(engine, items=items, marked=marked):
                last_wait = {}
                dirty = False
                for i, it in enumerate(items):
                    if it[0] == "op":
                        inst = it[1](engine)
                        if i in marked:
                            inst.then_inc(sems[it[2]], inc_of[it[2]])
                        dirty = True
                    elif it[0] == "drain":
                        if dirty:
                            engine.drain()
                            dirty = False
                    else:
                        tok = it[1]
                        val = cnt[tok.eng][tok.idx] * inc_of[tok.sem]
                        if last_wait.get(tok.sem, -1) < val:
                            engine.wait_ge(sems[tok.sem], val)
                            last_wait[tok.sem] = val
            getattr(block, e)(body)
        return stack


def build_nc(T=2048):
    NS = T // ST
    nc = bass.Bass("TRN2", target_bir_lowering=False, dynamic_dma_scratch_size=14336)

    x_d = nc.dram_tensor("x", [T, D], F32, kind="ExternalInput").ap()
    wf_d = nc.dram_tensor("wf_s", [2 * ECH, P, KC, P], F32, kind="ExternalInput").ap()
    wd_d = nc.dram_tensor("wd_s", [NDC, P, ECH, P], F32, kind="ExternalInput").ap()
    bf_d = nc.dram_tensor("bf", [P, 2 * ECH], F32, kind="ExternalInput").ap()
    w_d = nc.dram_tensor("w", [P, NDC, D], F32, kind="ExternalInput").ap()
    fb_d = nc.dram_tensor("fb", [P, NDC], F32, kind="ExternalInput").ap()
    idf_d = nc.dram_tensor("idf", [P, P], F32, kind="ExternalInput").ap()
    idb_d = nc.dram_tensor("idb", [P, P], BF16, kind="ExternalInput").ap()
    y_d = nc.dram_tensor("y", [T, D], F32, kind="ExternalOutput").ap()
    wfhl_d = nc.dram_tensor("wf_hl", [2 * ECH, P, 2 * KC * P], BF16, kind="Internal").ap()

    import contextlib
    alloc = contextlib.ExitStack()

    def sb(name, shape, dt=F32):
        return alloc.enter_context(nc.sbuf_tensor(name, shape, dt))

    def ps(name, shape, dt=F32):
        return alloc.enter_context(nc.psum_tensor(name, shape, dt))

    wq_sb = sb("wq_sb", [P, KC, NDC, P], BF16)
    wf_buf = sb("wf_buf", [P, 2, 2, KC, P], BF16)
    wd_buf = sb("wd_buf", [P, 2, ECH, P], F32)
    xin = sb("xin", [P, 2, NSUB * TC, D], F32)
    xn_tok = sb("xn_tok", [P, TC, D], F32)
    xn_fm = sb("xn_fm", [P, 2, KC, ST], F32)

    xq_fm = sb("xq_fm", [P, 2, KC, ST], BF16)
    a_buf = sb("a_buf", [P, ECH, ST], F32)
    b2 = sb("b2", [P, 2, ST], F32)
    gtmp = sb("gtmp", [P, 2, ST], F32)
    attn_fm = sb("attn_fm", [P, NDC, ST], F32)
    ffn_fm = sb("ffn_fm", [P, NDC, ST], F32)
    _ffl = ffn_fm[:].rearrange("p a b -> p (a b)")
    wfin2 = [_ffl[:, i * KC * P:(i + 1) * KC * P].rearrange(
        "p (k c) -> p k c", c=P) for i in range(2)]
    obuf = sb("obuf", [P, 3, D], F32)
    qs_w = sb("qs_w", [P, D], F32)
    qs_f = sb("qs_f", [P, D], F32)
    qs_fi = qs_f[:].bitcast(I32)
    stats = sb("stats", [P, 32], F32)
    bf_sb = sb("bf_sb", [P, 2 * ECH], F32)
    nbf_sb = sb("nbf_sb", [P, ECH], F32)
    bfh5_sb = sb("bfh5_sb", [P, ECH], F32)
    fb_sb = sb("fb_sb", [P, NDC], F32)
    wscale = sb("wscale", [P, NDC], F32)
    xscale = sb("xscale", [P, 2, NSUB * TC], F32)
    xinv = sb("xinv", [P, NSUB * TC], F32)
    ident_f = sb("ident_f", [P, P], F32)
    ident_b = sb("ident_b", [P, P], BF16)
    carry = sb("carry", [P, ECH], F32)
    c127 = sb("c127", [P, 1], F32)
    r127 = sb("r127", [P, 1], F32)
    ntmp = sb("ntmp", [P, NSUB * TC], F32)

    psum_m = [ps(f"psum_m{i}", [P, ST]) for i in range(4)]
    psum_f = [ps("psum_f0", [P, ST]), ps("psum_f1", [P, ST])]
    psum_t = [ps("psum_t0", [P, 512]), ps("psum_t1", [P, 512])]

    MSUM, QSUM, MEAN, VAR, INV, NMI, MX, MN = 0, 4, 8, 12, 16, 20, 24, 28
    AMAX, QT = 24, 26

    S = Sched()
    t_hist = [None, None]
    t_use = [0]

    def t_bank():
        b = t_use[0] % 2
        t_use[0] += 1
        return b

    m_hist = [None] * 4
    m_use = [0]

    def m_bank():
        b = m_use[0] % 4
        m_use[0] += 1
        return b

    qlast = {n: {"t": None, "v": None, "s": None, "g": None} for n in ("w", "i", "f")}

    def claim_bufs(names, writer_eng):
        for n in names:
            for t in qlast[n].values():
                S.wait(writer_eng, t)

    qbf = obuf[:].bitcast(BF16).rearrange("p a b -> p (a b)")
    xnfm_bf = xn_fm[:].bitcast(BF16)

    def qslot(i):
        return qbf[:, i * D:(i + 1) * D]

    S.dma("gpsimd", lambda g: g.dma_start(out=bf_sb[:], in_=bf_d), "dpre")
    S.dma("gpsimd", lambda g: g.dma_start(out=fb_sb[:], in_=fb_d), "dpre")
    S.dma("gpsimd", lambda g: g.dma_start(out=ident_f[:], in_=idf_d), "dpre")
    tok_pre = S.dma("gpsimd", lambda g: g.dma_start(out=ident_b[:], in_=idb_d), "dpre")

    S.op("scalar", lambda s: s.preload_act_table(AF.Sigmoid))

    S.op("vector", lambda v: v.memset(carry[:], 0.5))
    S.op("vector", lambda v: v.memset(c127[:], 127.0))
    S.drain("vector")
    S.op("vector", lambda v: v.reciprocal(r127[:], c127[:]))
    S.wait("vector", tok_pre)
    tok_nbf = S.op("vector", lambda v: v.tensor_scalar_mul(nbf_sb[:], bf_sb[:, 0:ECH], -1.0))
    tok_bfh5 = S.op("vector", lambda v: v.tensor_scalar_add(bfh5_sb[:], bf_sb[:, ECH:2 * ECH], 0.5))

    def x_dma(u):
        def fn(g):
            return g.dma_start(
                out=xin[:, u % 2, :, :],
                in_=x_d[u * ST:(u + 1) * ST, :].rearrange("(c p) d -> p c d", p=P))
        return fn
    tok_xdma = [None] * NS
    tok_xdma[0] = S.dma("gpsimd", x_dma(0), "dx0")

    tok_stt_last = [None] * NS
    tok_out_dma = {}
    tok_wf_mm = {}
    tok_gate_s = {}
    tok_gate_v = {}
    tok_lin_v = {}
    tok_scan = {}
    tok_carry = {}
    tok_wd_mm = {}
    tok_attn_ev = {}
    tok_ffn_mm = {}
    tok_ffn_ev = {}
    tok_outT_ffn_last = [None] * NS
    tok_outT_attn_last = [None] * NS
    tok_xnT_last = {}
    tok_xnfm_ev = [None] * NS
    tok_xqfm_ev = [None] * NS
    tok_xn_reader_v = {}
    tok_xn_last = {}
    tok_wq_ready = None
    tok_wscale_last = None
    tok_q = {}
    tok_xqT_last_super = {}
    tok_wqT_last = [None]
    slot_tlast = [None] * 6

    def stage_BC(u):
        pbuf = u % 2
        S.wait("vector", tok_xdma[u])
        if u > 0:
            S.wait("vector", tok_xn_last.get((u - 1, NSUB - 1)))
        for sc in range(NSUB * TC):
            xsrc = xin[:, pbuf, sc, :]
            S.op("vector", lambda v, xsrc=xsrc, sc=sc: v.reduce_sum(
                stats[:, MSUM + sc:MSUM + sc + 1], xsrc, axis=AX.X))
            S.op("vector", lambda v, xsrc=xsrc, sc=sc: v.reduce_max(
                stats[:, MX + sc:MX + sc + 1], xsrc, axis=AX.X))
            S.op("vector", lambda v, xsrc=xsrc, sc=sc: v.tensor_reduce(
                stats[:, MN + sc:MN + sc + 1], xsrc, axis=AX.X, op=ALU.min))
            if sc % 2:
                yield
        for sc in range(NSUB * TC):
            xsrc = xin[:, pbuf, sc, :]
            name = "w" if sc % 2 == 0 else "f"
            buf = qs_w if sc % 2 == 0 else qs_f
            claim_bufs([name], "scalar")
            S.wait("scalar", tok_xdma[u])
            tsq = S.op("scalar", lambda s, xsrc=xsrc, buf=buf: s.activation(buf[:], xsrc, AF.Square))
            qlast[name]["s"] = tsq
            S.wait("vector", tsq)
            tqs = S.op("vector", lambda v, sc=sc, buf=buf: v.reduce_sum(
                stats[:, QSUM + sc:QSUM + sc + 1], buf[:], axis=AX.X))
            qlast[name]["v"] = tqs
            yield

        def vop(fn):
            t = S.op("vector", fn)
            S.wait("vector", t)
            return t
        NC4 = NSUB * TC
        vop(lambda v: v.tensor_scalar_mul(stats[:, MEAN:MEAN + NC4], stats[:, MSUM:MSUM + NC4], 1.0 / D))
        vop(lambda v: v.tensor_scalar_mul(stats[:, QSUM:QSUM + NC4], stats[:, QSUM:QSUM + NC4], 1.0 / D))
        vop(lambda v: v.tensor_mul(stats[:, VAR:VAR + NC4], stats[:, MEAN:MEAN + NC4], stats[:, MEAN:MEAN + NC4]))
        vop(lambda v: v.tensor_sub(stats[:, VAR:VAR + NC4], stats[:, QSUM:QSUM + NC4], stats[:, VAR:VAR + NC4]))
        vop(lambda v: v.tensor_scalar_add(stats[:, VAR:VAR + NC4], stats[:, VAR:VAR + NC4], 1e-5))
        vop(lambda v: v.reciprocal(stats[:, INV:INV + NC4], stats[:, VAR:VAR + NC4]))
        yield
        for _ in range(4):
            vop(lambda v: v.tensor_mul(ntmp[:], stats[:, INV:INV + NC4], stats[:, INV:INV + NC4]))
            vop(lambda v: v.tensor_mul(ntmp[:], ntmp[:], stats[:, VAR:VAR + NC4]))
            vop(lambda v: v.tensor_scalar(ntmp[:], ntmp[:], -0.5, 1.5, op0=ALU.mult, op1=ALU.add))
            vop(lambda v: v.tensor_mul(stats[:, INV:INV + NC4], stats[:, INV:INV + NC4], ntmp[:]))
            yield
        tok_nmi = vop(lambda v: v.scalar_tensor_tensor(
            stats[:, NMI:NMI + NC4], stats[:, MEAN:MEAN + NC4], -1.0, stats[:, INV:INV + NC4],
            op0=ALU.mult, op1=ALU.mult))
        vop(lambda v: v.tensor_sub(stats[:, MX:MX + NC4], stats[:, MX:MX + NC4], stats[:, MEAN:MEAN + NC4]))
        vop(lambda v: v.tensor_sub(stats[:, MN:MN + NC4], stats[:, MEAN:MEAN + NC4], stats[:, MN:MN + NC4]))
        vop(lambda v: v.tensor_max(stats[:, MX:MX + NC4], stats[:, MX:MX + NC4], stats[:, MN:MN + NC4]))
        vop(lambda v: v.tensor_mul(stats[:, MX:MX + NC4], stats[:, MX:MX + NC4], stats[:, INV:INV + NC4]))
        vop(lambda v: v.tensor_scalar_max(stats[:, MX:MX + NC4], stats[:, MX:MX + NC4], 1e-8))
        tok_xs = vop(lambda v, pbuf=pbuf: v.tensor_scalar_mul(
            xscale[:, pbuf, 0:NC4], stats[:, MX:MX + NC4], r127[:]))
        vop(lambda v, pbuf=pbuf: v.reciprocal(xinv[:, 0:NC4], xscale[:, pbuf, 0:NC4]))
        vop(lambda v: v.tensor_mul(stats[:, MN:MN + NC4], stats[:, INV:INV + NC4], xinv[:, 0:NC4]))
        tok_qaff = vop(lambda v: v.scalar_tensor_tensor(
            stats[:, MSUM:MSUM + NC4], stats[:, NMI:NMI + NC4], 0.5, xinv[:, 0:NC4],
            op0=ALU.bypass, op1=ALU.mult))
        tok_qaff = vop(lambda v: v.tensor_scalar_add(
            stats[:, MSUM:MSUM + NC4], stats[:, MSUM:MSUM + NC4], 0.5))
        yield

        last_xn_ev = None
        last_xq_ev = None
        for sub in range(NSUB):
            prev_key = (u, sub - 1) if sub else (u - 1, NSUB - 1)
            for tc in range(TC):
                sc = sub * TC + tc
                xsrc = xin[:, pbuf, sc, :]
                col = sub * 256 + tc * P
                S.wait("scalar", tok_nmi)
                S.wait("scalar", tok_xnT_last.get(prev_key))
                tok_xn = S.op("scalar", lambda s, xsrc=xsrc, tc=tc, sc=sc: s.activation(
                    xn_tok[:, tc, :], xsrc, AF.Identity,
                    bias=stats[:, NMI + sc:NMI + sc + 1], scale=stats[:, INV + sc:INV + sc + 1]))
                tok_xn_last[(u, sub)] = tok_xn
                yield
                for hh in range(2):
                    bk = t_bank()
                    S.wait("tensor", tok_xn)
                    S.wait("tensor", tok_pre)
                    S.wait("tensor", t_hist[bk])
                    pbf = psum_t[bk][:]
                    tg = None
                    for s4 in range(4):
                        kc = hh * 4 + s4
                        tg = S.op("tensor", lambda t, pbf=pbf, s4=s4, kc=kc, tc=tc: t.transpose(
                            pbf[:, s4 * P:(s4 + 1) * P],
                            xn_tok[:, tc, kc * P:(kc + 1) * P], ident_f[:]))
                    tok_xnT_last[(u, sub)] = tg
                    S.wait("scalar", tg)
                    if u > 1 and sub == 0 and tc == 0 and hh == 0:
                        S.wait("scalar", tok_wf_mm[(u - 2, 2 * ECH - 1)])
                    hi_ap = xnfm_bf[:, pbuf, hh * 4:(hh + 1) * 4, col:col + P]
                    tok_ev = S.op("scalar", lambda s, pbf=pbf, hi_ap=hi_ap: s.copy(
                        hi_ap, pbf[:, 0:512].rearrange("p (a b) -> p a b", b=P)))
                    S.wait("vector", tok_ev)
                    lo_ap = xnfm_bf[:, pbuf, hh * 4:(hh + 1) * 4, ST + col:ST + col + P]
                    tok_lo = S.op("vector", lambda v, pbf=pbf, hi_ap=hi_ap, lo_ap=lo_ap: v.tensor_tensor(
                        lo_ap, pbf[:, 0:512].rearrange("p (a b) -> p a b", b=P),
                        hi_ap, op=ALU.subtract))
                    t_hist[bk] = tok_lo
                    last_xn_ev = tok_lo
                    yield
        for sc in range(NSUB * TC):
            xsrc = xin[:, pbuf, sc, :]
            s2c = stats[:, MN + sc:MN + sc + 1]
            b5c = stats[:, MSUM + sc:MSUM + sc + 1]
            claim_bufs(["w", "f"], "vector")
            S.wait("vector", tok_qaff)
            S.op("vector", lambda v, xsrc=xsrc, s2c=s2c, b5c=b5c: v.tensor_scalar(
                qs_w[:], xsrc, s2c, b5c, op0=ALU.mult, op1=ALU.add))
            S.op("vector", lambda v, xsrc=xsrc, s2c=s2c, b5c=b5c: v.tensor_scalar(
                qs_fi[:], xsrc, s2c, b5c, op0=ALU.mult, op1=ALU.add))
            S.drain("vector")
            S.op("vector", lambda v: v.tensor_tensor(
                qs_w[:], qs_w[:], qs_fi[:], op=ALU.is_lt))
            S.drain("vector")
            S.wait("vector", slot_tlast[sc])
            if u > 0:
                pck = (u - 1) * 4 + sc
                if pck in tok_out_dma:
                    S.wait("vector", tok_out_dma[pck])
            tok_q[(u, sc)] = S.op("vector", lambda v, sc=sc: v.tensor_tensor(
                qslot(sc), qs_fi[:], qs_w[:], op=ALU.subtract))
            for n in ("w", "f"):
                qlast[n]["v"] = tok_q[(u, sc)]
            yield
        for sc in range(NSUB * TC):
            col = sc * P
            for hh in range(2):
                bk = t_bank()
                S.wait("tensor", tok_q[(u, sc)])
                S.wait("tensor", t_hist[bk])
                pb = psum_t[bk][:].bitcast(BF16)
                tg = None
                for s4 in range(4):
                    kc = hh * 4 + s4
                    tg = S.op("tensor", lambda t, pb=pb, s4=s4, kc=kc, sc=sc: t.transpose(
                        pb[:, s4 * P:(s4 + 1) * P],
                        qslot(sc)[:, kc * P:(kc + 1) * P], ident_b[:]))
                tok_xqT_last_super[u] = tg
                slot_tlast[sc] = tg
                S.wait("scalar", tg)
                if u > 1 and sc == 0 and hh == 0:
                    S.wait("scalar", tok_ffn_mm[(u - 2, NDC - 1)])
                tok_ev = S.op("scalar", lambda s, pb=pb, hh=hh, col=col, pbuf=pbuf: s.copy(
                    xq_fm[:, pbuf, hh * 4:(hh + 1) * 4, col:col + P],
                    pb[:, 0:512].rearrange("p (a b) -> p a b", b=P)))
                t_hist[bk] = tok_ev
                last_xq_ev = tok_ev
                yield
        tok_xnfm_ev[u] = last_xn_ev
        tok_xqfm_ev[u] = last_xq_ev

    tok_wsplit = {}
    tok_wsout = {}

    def stage_D(u, feed):
        pbuf = u % 2
        for j in range(2 * ECH):
            feed(2 if u == 0 else 1)
            sg = u * 2 * ECH + j
            e2 = j // 2
            par = j % 2
            if u == 0:
                wfin = wfin2[par]
                if sg >= 2:
                    S.wait("sync", tok_wsplit[sg - 2])
                tok_dwf = S.dma("sync", lambda s, j=j, wfin=wfin: s.dma_start(
                    out=wfin, in_=wf_d[ECS[j]]), f"dwin{par}")
                if sg >= 2:
                    pi, pj = divmod(sg - 2, 2 * ECH)
                    S.wait("gpsimd", tok_wf_mm[(pi, pj)])
                    S.wait("gpsimd", tok_wsout[j - 2])
                tcw = S.dma("gpsimd", lambda g, j=j, par=par: g.dma_start(
                    out=wf_buf[:, par, 0, :, :], in_=wf_d[ECS[j]]), f"dcw{par}")
                S.wait("vector", tcw)
                S.wait("vector", tok_dwf)
                tsp = S.op("vector", lambda v, par=par, wfin=wfin: v.tensor_tensor(
                    wf_buf[:, par, 1, :, :], wfin, wf_buf[:, par, 0, :, :],
                    op=ALU.subtract))
                tok_wsplit[sg] = tsp
                S.wait("sync", tsp)
                tok_wsout[j] = S.dma("sync", lambda s, j=j, par=par: s.dma_start(
                    out=wfhl_d[ECS[j]], in_=wf_buf[:, par, :, :, :].rearrange(
                        "p a b c -> p (a b c)")), f"dws{par}")
                tok_wrdy = tsp
            else:
                if sg >= 2:
                    pi, pj = divmod(sg - 2, 2 * ECH)
                    S.wait("sync", tok_wf_mm[(pi, pj)])
                if u == 1:
                    S.wait("sync", tok_wsout[2 * ECH - 2 + par])
                tok_wrdy = S.dma("sync", lambda s, j=j, par=par: s.dma_start(
                    out=wf_buf[:, par, :, :, :].rearrange("p a b c -> p (a b c)"),
                    in_=wfhl_d[ECS[j]]), f"dwf{par}")
            mb = m_bank()
            if u == 0:
                S.wait("tensor", tcw)
                S.wait("tensor", tok_xnfm_ev[u])
                S.wait("tensor", m_hist[mb])
                for kc in range(KC):
                    S.op("tensor", lambda t, kc=kc, mb=mb, pbuf=pbuf, par=par: t.matmul(
                        psum_m[mb][:], wf_buf[:, par, 0, kc, :],
                        xnfm_bf[:, pbuf, kc, 0:ST],
                        start=(kc == 0), stop=False))
                S.wait("tensor", tok_wrdy)
                tg = None
                for kc in range(KC):
                    for (wl, xl) in ((0, 1), (1, 0)):
                        tg = S.op("tensor", lambda t, kc=kc, mb=mb, pbuf=pbuf, par=par, wl=wl, xl=xl: t.matmul(
                            psum_m[mb][:], wf_buf[:, par, wl, kc, :],
                            xnfm_bf[:, pbuf, kc, xl * ST:(xl + 1) * ST],
                            start=False,
                            stop=(kc == KC - 1 and wl == 1)))
            else:
                S.wait("tensor", tok_wrdy)
                S.wait("tensor", tok_xnfm_ev[u])
                S.wait("tensor", m_hist[mb])
                tg = None
                for kc in range(KC):
                    for (wl, xl) in ((0, 0), (0, 1), (1, 0)):
                        tg = S.op("tensor", lambda t, kc=kc, mb=mb, pbuf=pbuf, par=par, wl=wl, xl=xl: t.matmul(
                            psum_m[mb][:], wf_buf[:, par, wl, kc, :],
                            xnfm_bf[:, pbuf, kc, xl * ST:(xl + 1) * ST],
                            start=(kc == 0 and wl == 0 and xl == 0),
                            stop=(kc == KC - 1 and wl == 1)))
            tok_wf_mm[(u, j)] = tg
            S.wait("scalar", tg)
            if j % 2 == 0:
                bpar = e2 % 2
                if u > 0:
                    S.wait("scalar", tok_carry[u - 1])
                    S.wait("scalar", tok_wd_mm[(u - 1, NDC - 1)])
                if e2 >= 2:
                    S.wait("scalar", tok_scan[(u, e2 - 2)])
                elif u > 0:
                    S.wait("scalar", tok_scan[(u - 1, ECH - 2 + e2)])
                S.wait("scalar", tok_pre)
                S.wait("scalar", tok_nbf)
                S.op("scalar", lambda s, mb=mb, e2=e2, bpar=bpar: s.activation(
                    b2[:, bpar, :], psum_m[mb][:], AF.Sigmoid, bias=bf_sb[:, e2:e2 + 1]))
                tok_gate_s[(u, j)] = S.op("scalar", lambda s, mb=mb, e2=e2: s.activation(
                    a_buf[:, e2, :], psum_m[mb][:], AF.Sigmoid,
                    bias=nbf_sb[:, e2:e2 + 1], scale=-1.0))
            else:
                q = e2 % 2
                bpar = e2 % 2
                if e2 >= 2:
                    S.wait("scalar", tok_gate_v[(u, e2 - 2)])
                elif u > 0:
                    S.wait("scalar", tok_gate_v[(u - 1, ECH - 2 + e2)])
                tok_gate_s[(u, j)] = S.op("scalar", lambda s, mb=mb, q=q, e2=e2: s.activation(
                    gtmp[:, q, :], psum_m[mb][:], AF.Sigmoid, bias=bf_sb[:, ECH + e2:ECH + e2 + 1]))
                S.wait("vector", tok_gate_s[(u, j)])
                S.wait("vector", tok_bfh5)
                tv1 = S.op("vector", lambda v, mb=mb, q=q, e2=e2: v.scalar_tensor_tensor(
                    gtmp[:, q, :], psum_m[mb][:], bfh5_sb[:, e2:e2 + 1], gtmp[:, q, :],
                    op0=ALU.add, op1=ALU.max))
                tok_lin_v[(u, e2)] = tv1
                S.wait("vector", tv1)
                S.wait("vector", tok_gate_s[(u, j - 1)])
                tok_gate_v[(u, e2)] = S.op("vector", lambda v, q=q, e2=e2, bpar=bpar: v.tensor_mul(
                    b2[:, bpar, :], b2[:, bpar, :], gtmp[:, q, :]))
                S.wait("vector", tok_gate_v[(u, e2)])
                if u > 0:
                    S.wait("vector", tok_wd_mm[(u - 1, NDC - 1)])
                    S.wait("vector", tok_carry[u - 1])
                tok_scan[(u, e2)] = S.op("vector", lambda v, e2=e2, bpar=bpar: v.tensor_tensor_scan(
                    a_buf[:, e2, :], a_buf[:, e2, :], b2[:, bpar, :], carry[:, e2:e2 + 1],
                    op0=ALU.mult, op1=ALU.add))
                if e2 == ECH - 1:
                    S.wait("vector", tok_scan[(u, e2)])
                    tok_carry[u] = S.op("vector", lambda v: v.tensor_copy(
                        carry[:, 0:ECH],
                        a_buf[:, :, ST - 1:ST].rearrange("p a b -> p (a b)")))
            m_hist[mb] = tok_gate_s[(u, j)] if j % 2 == 0 else tok_lin_v[(u, e2)]

    def stage_WQ():
        nonlocal tok_wq_ready, tok_wscale_last
        xw4 = xin[:, 1, :, :]
        aw4 = attn_fm[:].rearrange("p a b -> p (a b)").rearrange(
            "p (a b) -> p a b", b=D)
        tok_dw0 = S.dma("sync", lambda e: e.dma_start(
            out=xw4, in_=w_d[:, 0:4, :]), "dwq0")
        tok_dw1 = S.dma("sync", lambda e: e.dma_start(
            out=aw4, in_=w_d[:, 4:8, :]), "dwq1")
        last_w_reader = None
        WV = 4
        S.wait("vector", tok_dw0)
        S.wait("vector", tok_dw1)
        for dc in range(NDC):
            wsrc = (xw4 if dc < 4 else aw4)[:, dc % 4, :]
            S.op("vector", lambda v, wsrc=wsrc, dc=dc: v.reduce_max(
                stats[:, WV + dc:WV + dc + 1], wsrc, axis=AX.X, apply_absolute_value=True))
            if dc % 4 == 3:
                yield
        S.drain("vector")
        t2 = S.op("vector", lambda v: v.tensor_scalar_max(
            stats[:, WV:WV + NDC], stats[:, WV:WV + NDC], 1e-8))
        S.wait("vector", t2)
        t3 = S.op("vector", lambda v: v.tensor_scalar_mul(
            wscale[:, 0:NDC], stats[:, WV:WV + NDC], r127[:]))
        tok_wscale_last = t3
        S.wait("vector", t3)
        t4 = S.op("vector", lambda v: v.reciprocal(
            stats[:, WV:WV + NDC], wscale[:, 0:NDC]))
        S.wait("vector", t4)
        yield
        for dc in range(NDC):
            wsrc = (xw4 if dc < 4 else aw4)[:, dc % 4, :]
            winv = stats[:, WV + dc:WV + dc + 1]
            claim_bufs(["w", "f"], "vector")
            S.op("vector", lambda v, wsrc=wsrc, winv=winv: v.tensor_scalar(
                qs_w[:], wsrc, winv, 0.5, op0=ALU.mult, op1=ALU.add))
            ti = S.op("vector", lambda v, wsrc=wsrc, winv=winv: v.tensor_scalar(
                qs_fi[:], wsrc, winv, 0.5, op0=ALU.mult, op1=ALU.add))
            last_w_reader = ti
            S.drain("vector")
            S.op("vector", lambda v: v.tensor_tensor(
                qs_w[:], qs_w[:], qs_fi[:], op=ALU.is_lt))
            S.drain("vector")
            wslot = dc % 6
            S.wait("vector", slot_tlast[wslot])
            tq = S.op("vector", lambda v, wslot=wslot: v.tensor_tensor(
                qslot(wslot), qs_fi[:], qs_w[:], op=ALU.subtract))
            for n in ("w", "f"):
                qlast[n]["v"] = tq
            yield
            for hh in range(2):
                bk = t_bank()
                S.wait("tensor", tq)
                S.wait("tensor", t_hist[bk])
                pb = psum_t[bk][:].bitcast(BF16)
                tg = None
                for s4 in range(4):
                    kc = hh * 4 + s4
                    tg = S.op("tensor", lambda t, pb=pb, s4=s4, kc=kc, wslot=wslot: t.transpose(
                        pb[:, s4 * P:(s4 + 1) * P],
                        qslot(wslot)[:, kc * P:(kc + 1) * P], ident_b[:]))
                slot_tlast[wslot] = tg
                tok_wqT_last[0] = tg
                S.wait("scalar", tg)
                tok_ev = S.op("scalar", lambda s, pb=pb, hh=hh, dc=dc: s.copy(
                    wq_sb[:, hh * 4:(hh + 1) * 4, dc, :],
                    pb[:, 0:512].rearrange("p (a b) -> p a b", b=P)))
                t_hist[bk] = tok_ev
                tok_wq_ready = tok_ev
                yield
        if NS > 1:
            S.wait("gpsimd", last_w_reader)
            tok_xdma[1] = S.dma("gpsimd", x_dma(1), "dx1")

    def stage_F(u, feed):
        pbuf = u % 2
        for dc in range(NDC):
            feed(6)
            fpar = dc % 2
            S.wait("tensor", tok_xqfm_ev[u])
            S.wait("tensor", tok_wq_ready)
            if dc >= 2:
                S.wait("tensor", tok_ffn_ev[(u, dc - 2)])
            elif u > 0:
                S.wait("tensor", tok_ffn_ev[(u - 1, NDC - 2 + dc)])
            tg = None
            for kc in range(KC):
                tg = S.op("tensor", lambda t, fpar=fpar, kc=kc, dc=dc, pbuf=pbuf: t.matmul(
                    psum_f[fpar][:], wq_sb[:, kc, dc, :], xq_fm[:, pbuf, kc, :],
                    start=(kc == 0), stop=(kc == KC - 1)))
            tok_ffn_mm[(u, dc)] = tg
            S.wait("scalar", tg)
            S.wait("scalar", tok_wscale_last)
            if u > 0:
                S.wait("scalar", tok_outT_ffn_last[u - 1])
            tok_ffn_ev[(u, dc)] = S.op("scalar", lambda s, fpar=fpar, dc=dc: s.activation(
                ffn_fm[:, dc, :], psum_f[fpar][:], AF.Copy, scale=wscale[:, dc:dc + 1]))

    def stage_E(u, feed):
        for dc in range(NDC):
            sgd = u * NDC + dc
            if sgd >= 2:
                pi, pdc = divmod(sgd - 2, NDC)
                S.wait("sync", tok_wd_mm[(pi, pdc)])
            tok_dwd = S.dma("sync", lambda s, dc=dc: s.dma_start(
                out=wd_buf[:, dc % 2, :, :], in_=wd_d[dc]), f"dwd{dc % 2}")
            mb = m_bank()
            S.wait("tensor", tok_dwd)
            S.wait("tensor", tok_carry[u])
            S.wait("tensor", m_hist[mb])
            tg = None
            for e2 in range(ECH):
                tg = S.op("tensor", lambda t, dc=dc, e2=e2, mb=mb: t.matmul(
                    psum_m[mb][:], wd_buf[:, dc % 2, e2, :], a_buf[:, e2, :],
                    start=(e2 == 0), stop=(e2 == ECH - 1)))
            tok_wd_mm[(u, dc)] = tg
            S.wait("scalar", tg)
            S.wait("scalar", tok_pre)
            if u > 0:
                S.wait("scalar", tok_outT_attn_last[u - 1])
            tok_attn_ev[(u, dc)] = S.op("scalar", lambda s, mb=mb, dc=dc: s.activation(
                attn_fm[:, dc, :], psum_m[mb][:], AF.Identity, bias=fb_sb[:, dc:dc + 1]))
            m_hist[mb] = tok_attn_ev[(u, dc)]

    def stage_G(u):
        pbuf = u % 2
        for sub in range(NSUB):
            for tc in range(TC):
                col = sub * 256 + tc * P
                sc = sub * TC + tc
                ck = u * 4 + sub * TC + tc
                r = ck % 3
                tok_add = None
                for j2 in range(2):
                    bk = t_bank()
                    S.wait("tensor", tok_ffn_ev[(u, j2 * 4 + 3)])
                    S.wait("tensor", t_hist[bk])
                    tgf = None
                    for s4 in range(4):
                        dc = j2 * 4 + s4
                        tgf = S.op("tensor", lambda t, bk=bk, s4=s4, dc=dc, col=col: t.transpose(
                            psum_t[bk][:, s4 * P:(s4 + 1) * P],
                            ffn_fm[:, dc, col:col + P], ident_f[:]))
                    tok_outT_ffn_last[u] = tgf
                    S.wait("vector", tgf)
                    if ck >= 3:
                        S.wait("vector", tok_out_dma[ck - 3])
                    S.wait("vector", tok_xqT_last_super.get(u + 1))
                    S.wait("vector", tok_wqT_last[0])
                    tok_stt = S.op("vector", lambda v, bk=bk, r=r, j2=j2, pbuf=pbuf, sub=sub, tc=tc, sc=sc: v.scalar_tensor_tensor(
                        obuf[:, r, j2 * 512:(j2 + 1) * 512], psum_t[bk][:],
                        xscale[:, pbuf, sc:sc + 1], xin[:, pbuf, sub * TC + tc, j2 * 512:(j2 + 1) * 512],
                        op0=ALU.mult, op1=ALU.add))
                    t_hist[bk] = tok_stt
                    tok_stt_last[u] = tok_stt
                    bk2 = t_bank()
                    S.wait("tensor", tok_attn_ev[(u, j2 * 4 + 3)])
                    S.wait("tensor", t_hist[bk2])
                    tga = None
                    for s4 in range(4):
                        dc = j2 * 4 + s4
                        tga = S.op("tensor", lambda t, bk2=bk2, s4=s4, dc=dc, col=col: t.transpose(
                            psum_t[bk2][:, s4 * P:(s4 + 1) * P],
                            attn_fm[:, dc, col:col + P], ident_f[:]))
                    tok_outT_attn_last[u] = tga
                    S.wait("vector", tga)
                    S.wait("vector", tok_stt)
                    tok_add = S.op("vector", lambda v, bk2=bk2, r=r, j2=j2: v.tensor_add(
                        obuf[:, r, j2 * 512:(j2 + 1) * 512],
                        obuf[:, r, j2 * 512:(j2 + 1) * 512], psum_t[bk2][:]))
                    t_hist[bk2] = tok_add
                S.wait("gpsimd", tok_add)
                t0 = u * ST + sub * 256 + tc * P
                tok_out_dma[ck] = S.dma("gpsimd", lambda g, t0=t0, r=r: g.dma_start(
                    out=y_d[t0:t0 + P, :], in_=obuf[:, r, :]), f"dout{r}")

    filler = []

    def feed(n):
        while n > 0 and filler:
            try:
                next(filler[0])
                n -= 1
            except StopIteration:
                filler.pop(0)

    def exhaust():
        while filler:
            try:
                next(filler[0])
            except StopIteration:
                filler.pop(0)

    g = stage_BC(0)
    for _ in g:
        pass
    for u in range(NS):
        if u == 0:
            filler.append(stage_WQ())
        if u + 1 < NS:
            filler.append(stage_BC(u + 1))
        stage_D(u, feed)
        if u == 0 and filler:
            wq_gen = filler[0]
            for _ in wq_gen:
                pass
            filler.pop(0)
        stage_F(u, feed)
        stage_E(u, feed)
        exhaust()
        stage_G(u)
        if u + 2 < NS:
            S.wait("gpsimd", tok_stt_last[u])
            tok_xdma[u + 2] = S.dma("gpsimd", x_dma(u + 2), f"dx{(u + 2) % 2}")

    for ck in range(max(0, NS * 4 - 3), NS * 4):
        S.wait("gpsimd", tok_out_dma[ck])

    with nc.Block() as block:
        guard = S.emit(nc, block)
    guard.close()
    alloc.close()
    return nc


def _layout(L0_attn_x, L0_attn_W_f, L0_attn_b_f, L0_attn_W_down, L0_ffn_weight, L0_ffn_bias):
    x = np.ascontiguousarray(np.asarray(L0_attn_x, dtype=np.float32))
    wf = np.asarray(L0_attn_W_f, dtype=np.float32)
    bfv = np.asarray(L0_attn_b_f, dtype=np.float32)
    wd = np.asarray(L0_attn_W_down, dtype=np.float32)
    w = np.asarray(L0_ffn_weight, dtype=np.float32)
    fb = np.asarray(L0_ffn_bias, dtype=np.float32)

    wf_s = np.ascontiguousarray(wf.T.reshape(KC, P, 2 * ECH, P).transpose(2, 1, 0, 3))
    wd_s = np.ascontiguousarray(wd.T.reshape(ECH, P, NDC, P).transpose(2, 1, 0, 3))
    bf_h = np.ascontiguousarray(bfv.reshape(2 * ECH, P).T)
    w_h = np.ascontiguousarray(w.reshape(NDC, P, D).transpose(1, 0, 2))
    fb_h = np.ascontiguousarray(fb.reshape(NDC, P).T)
    idf = np.eye(P, dtype=np.float32)
    idb = np.eye(P, dtype=ml_dtypes.bfloat16)
    return x, wf_s, wd_s, bf_h, w_h, fb_h, idf, idb


def kernel(L0_attn_x, L0_attn_W_f, L0_attn_b_f, L0_attn_W_down, L0_ffn_weight, L0_ffn_bias):
    x, wf_s, wd_s, bf_h, w_h, fb_h, idf, idb = _layout(
        L0_attn_x, L0_attn_W_f, L0_attn_b_f, L0_attn_W_down, L0_ffn_weight, L0_ffn_bias)
    B, T, _ = x.shape
    nc = build_nc(T)
    in_maps = [dict(x=np.ascontiguousarray(x[b]), wf_s=wf_s, wd_s=wd_s, bf=bf_h,
                    w=w_h, fb=fb_h, idf=idf, idb=idb) for b in range(B)]
    res = run_bass_kernel_spmd(nc, in_maps, core_ids=list(range(B)))
    return np.stack([res.results[b]["y"] for b in range(B)], axis=0)
